# version 04
# baseline (speedup 1.0000x reference)
import numpy as np
import ml_dtypes
from contextlib import ExitStack

import concourse.bass as bass
import concourse.mybir as mybir
from concourse.bass_utils import run_bass_kernel_spmd

F32 = mybir.dt.float32
BF16 = mybir.dt.bfloat16
I32 = mybir.dt.int32
AF = mybir.ActivationFunctionType
ALU = mybir.AluOpType
AX = mybir.AxisListType
BFD = ml_dtypes.bfloat16

D = 1024
T = 4096
TH = 2048
H0, HD0, LAT = 8, 256, 64
H1, HD1 = 16, 64
DH0, DH1 = 4096, 3072
EPS = 1e-5
MARGIN = 4.0
SC0 = 1.0 / 16.0
QS1 = 0.125
C127 = 1.0 / 127.0



def _pret(wT, nic, noc):
    a = wT.reshape(nic, 128, noc, 128).transpose(2, 1, 0, 3)
    return np.ascontiguousarray(a.reshape(noc * 128, nic * 128))


def _rope_tables(pos, scale):
    inv = 1.0 / (10000.0 ** (np.arange(0, HD1, 2, dtype=np.float64) / HD1))
    ang = np.outer(inv, pos.astype(np.float64))
    c32 = np.cos(ang)
    s32 = np.sin(ang)
    C = np.tile(c32, (4, 1)) * scale
    S = np.concatenate([-s32, s32, -s32, s32], axis=0) * scale
    return (np.ascontiguousarray(C.astype(np.float32)),
            np.ascontiguousarray(S.astype(np.float32)))


def _prep(inputs):
    f32 = np.float32
    x = np.asarray(inputs["L0_attn_x"], dtype=f32)

    WqT = np.ascontiguousarray(np.asarray(inputs["L0_attn_Wq"], f32).T)
    WkvdT = np.asarray(inputs["L0_attn_Wkv_down"], f32).T
    WkupT = np.ascontiguousarray(np.asarray(inputs["L0_attn_Wk_up"], f32).T)
    WvupT = np.ascontiguousarray(np.asarray(inputs["L0_attn_Wv_up"], f32).T)
    WoT = np.ascontiguousarray(np.asarray(inputs["L0_attn_Wo"], f32).T)

    wq_p = _pret(WqT, 8, 16)
    wkvd_p = np.ascontiguousarray(
        WkvdT.reshape(8, 128, 64).transpose(1, 0, 2).reshape(128, 512))

    w1_p = _pret(np.asarray(inputs["L0_ffn_w1_w"], f32).T, 8, 32)
    w2_p = _pret(np.asarray(inputs["L0_ffn_w2_w"], f32).T, 8, 32)
    h1_p = _pret(np.asarray(inputs["L0_ffn_h1_w"], f32).T, 32, 32)
    h2_p = _pret(np.asarray(inputs["L0_ffn_h2_w"], f32).T, 32, 32)
    w3_p = _pret(np.asarray(inputs["L0_ffn_w3_w"], f32).T, 32, 8)
    s1_t = np.ascontiguousarray(np.asarray(inputs["L0_ffn_w1_s"], f32).reshape(32, 128).T)
    s2_t = np.ascontiguousarray(np.asarray(inputs["L0_ffn_w2_s"], f32).reshape(32, 128).T)
    h1s_t = np.ascontiguousarray(np.asarray(inputs["L0_ffn_h1_s"], f32).reshape(32, 128).T)
    h2s_t = np.ascontiguousarray(np.asarray(inputs["L0_ffn_h2_s"], f32).reshape(32, 128).T)
    w3s_t = np.ascontiguousarray(np.asarray(inputs["L0_ffn_w3_s"], f32).reshape(8, 128).T)

    j = np.arange(64)
    de = np.where(j < 32, 2 * j, 2 * (j - 32) + 1)
    perm1 = (np.arange(H1)[:, None] * 64 + de[None, :]).reshape(-1)
    sw = (np.arange(64) + 32) % 64
    swapidx = (np.arange(H1)[:, None] * 64 + sw[None, :]).reshape(-1)

    wq1T = np.asarray(inputs["L1_attn_wq"], f32).T[:, perm1]
    wk1T = np.asarray(inputs["L1_attn_wk"], f32).T[:, perm1]
    wq1_p = _pret(np.ascontiguousarray(wq1T), 8, 8)
    wq1s_p = _pret(np.ascontiguousarray(wq1T[:, swapidx]), 8, 8)
    wk1_p = _pret(np.ascontiguousarray(wk1T), 8, 8)
    wk1s_p = _pret(np.ascontiguousarray(wk1T[:, swapidx]), 8, 8)
    wv1_t = np.ascontiguousarray(np.asarray(inputs["L1_attn_wv"], f32).T)
    wo1_t = np.ascontiguousarray(np.asarray(inputs["L1_attn_wo"], f32).T)

    wg_p = _pret(np.ascontiguousarray(np.asarray(inputs["L1_ffn_w_gate"], np.int32).T), 8, 24)
    wu_p = _pret(np.ascontiguousarray(np.asarray(inputs["L1_ffn_w_up"], np.int32).T), 8, 24)
    wd_p = _pret(np.ascontiguousarray(np.asarray(inputs["L1_ffn_w_down"], np.int32).T), 24, 8)

    ck_t, sk_t = _rope_tables(np.arange(T), 1.0)
    ones_col = np.ones((128, 1), dtype=BFD)
    ones_row = np.ones((1, 128), dtype=BFD)

    shared = dict(
        wq_p=wq_p, wkvd_p=wkvd_p, wkup=WkupT, wvup=WvupT, wo_t=WoT,
        w1_p=w1_p, w2_p=w2_p, h1_p=h1_p, h2_p=h2_p, w3_p=w3_p,
        s1_t=s1_t, s2_t=s2_t, h1s_t=h1s_t, h2s_t=h2s_t, w3s_t=w3s_t,
        wq1_p=wq1_p, wq1s_p=wq1s_p, wk1_p=wk1_p, wk1s_p=wk1s_p,
        wv1_t=wv1_t, wo1_t=wo1_t, wg_p=wg_p, wu_p=wu_p, wd_p=wd_p,
        ck_t=ck_t, sk_t=sk_t, ones_col=ones_col, ones_row=ones_row,
    )

    in_maps = []
    for c in range(8):
        bb, hh = c // 2, c % 2
        xT = x[bb].T
        own = np.ascontiguousarray(xT[:, hh * TH:(hh + 1) * TH])
        oth = np.ascontiguousarray(xT[:, (1 - hh) * TH:(2 - hh) * TH])
        cq_t, sq_t = _rope_tables(hh * TH + np.arange(TH), QS1)
        m = dict(shared)
        m["xt_full"] = np.ascontiguousarray(np.concatenate([own, oth], axis=1))
        m["xt_own"] = own
        m["cq_t"] = cq_t
        m["sq_t"] = sq_t
        in_maps.append(m)
    return in_maps



class Sem:
    def __init__(self, h):
        self.h = h
        self.n = 0


class Ph:
    ENG = ("sync", "gpsimd", "scalar", "vector", "tensor")

    def __init__(self):
        self.q = {k: [] for k in self.ENG}
        self.ni = 0

    def emit(self, eng, op, args, kw=None, waits=(), inc=None, inck=1):
        self.ni += 1
        if inc is not None:
            inc.n += inck
        h = None if inc is None else inc.h
        wl = tuple((s.h, int(v)) for (s, v) in waits if v)

        def f(e, op=op, args=tuple(args), kw=dict(kw or {}), wl=wl, h=h, inck=inck):
            for sh, v in wl:
                e.wait_ge(sh, v)
            r = getattr(e, op)(*args, **kw)
            if h is not None:
                r.then_inc(h, inck)
        self.q[eng].append(f)
        return None if inc is None else inc.n

    def wait(self, eng, waits):
        wl = tuple((s.h, int(v)) for (s, v) in waits if v)

        def f(e, wl=wl):
            for sh, v in wl:
                e.wait_ge(sh, v)
        self.q[eng].append(f)

    def run(self, nc):
        with nc.Block() as blk:
            for name in self.ENG:
                fns = self.q[name]
                if fns:
                    def runner(e, fns=fns):
                        for f in fns:
                            f(e)
                    getattr(blk, name)(runner)


class B:
    def __init__(self, nc, ctx):
        self.nc = nc
        self.tT = Sem(ctx.enter_context(nc.semaphore("tT")))
        self.tV = Sem(ctx.enter_context(nc.semaphore("tV")))
        self.tA = Sem(ctx.enter_context(nc.semaphore("tA")))
        self.dG = Sem(ctx.enter_context(nc.semaphore("dG")))
        self.dS = Sem(ctx.enter_context(nc.semaphore("dS")))
        self.dT = Sem(ctx.enter_context(nc.semaphore("dT")))
        self.cC = Sem(ctx.enter_context(nc.semaphore("cC")))
        self.ph = None
        self.resd_w = []
        self.d_ones = 0
        self.ninstr = 0
        self._ctx = ctx
        self._xsems = {}

    def sem(self, name):
        s = self._xsems.get(name)
        if s is None:
            s = Sem(self._ctx.enter_context(self.nc.semaphore("x_" + name)))
            self._xsems[name] = s
        return s

    def phase(self):
        self.ph = Ph()
        return self.ph

    def endphase(self):
        self.ph.run(self.nc)
        self.ninstr += self.ph.ni
        self.ph = None

    def mm(self, out, lhsT, rhs, start, stop, waits=(), inc=False, tp=None):
        kw = dict(start=start, stop=stop)
        if tp is not None:
            kw["tile_position"] = tp
        return self.ph.emit("tensor", "matmul", (out, lhsT, rhs), kw,
                            waits=waits, inc=self.tT if inc else None)

    def act(self, out, in_, func, bias=0.0, scale=1.0, accum=None, waits=()):
        kw = dict(bias=bias, scale=scale)
        if accum is not None:
            kw["accum_out"] = accum
        return self.ph.emit("scalar", "activation", (out, in_, func), kw,
                            waits=waits, inc=self.tA)

    def smul(self, out, in_, c, waits=()):
        return self.ph.emit("scalar", "mul", (out, in_, c), None,
                            waits=waits, inc=self.tA)

    def scopy(self, out, in_, waits=()):
        return self.ph.emit("scalar", "copy", (out, in_), None,
                            waits=waits, inc=self.tA)

    def ts(self, out, in0, s1, s2, op0, waits=(), inc=True):
        return self.ph.emit("vector", "tensor_scalar", (out, in0, s1, s2, op0), None,
                            waits=waits, inc=self.tV if inc else None)

    def tt(self, out, in0, in1, op, waits=(), inc=True):
        return self.ph.emit("vector", "tensor_tensor", (out, in0, in1, op), None,
                            waits=waits, inc=self.tV if inc else None)

    def vred(self, out, in_, op, negate=False, waits=(), inc=True):
        return self.ph.emit("vector", "tensor_reduce", (out, in_),
                            dict(axis=AX.X, op=op, negate=negate),
                            waits=waits, inc=self.tV if inc else None)

    def vcopy(self, out, in_, waits=(), inc=True):
        return self.ph.emit("vector", "tensor_copy", (out, in_), None,
                            waits=waits, inc=self.tV if inc else None)

    def vrecip(self, out, in_, waits=(), inc=True):
        return self.ph.emit("vector", "reciprocal", (out, in_), None,
                            waits=waits, inc=self.tV if inc else None)

    def dma(self, queue, out, in_, waits=(), accum=None, sem=None):
        if sem is None:
            sem = {"gpsimd": self.dG, "scalar": self.dS, "sync": self.dT}[queue]
        elif isinstance(sem, str):
            sem = self.sem(sem)
        kw = dict(out=out, in_=in_)
        if accum is not None:
            kw["accum_op"] = accum
        return self.ph.emit(queue, "dma_start", (), kw, waits=waits, inc=sem, inck=16)

    def tr(self, out, in_, waits=()):
        waits = list(waits) + [(self.dT, self.dT.n)]
        return self.ph.emit("sync", "dma_start_transpose", (out, in_), None,
                            waits=waits, inc=self.dT, inck=16)

    def coll(self, kind, op, groups, ins, outs, waits=()):
        return self.ph.emit("gpsimd", "collective_compute", (kind, op), dict(
            replica_groups=groups, ins=ins, outs=outs),
            waits=waits, inc=self.cC)


def _build(stop=None, flags=()):
    flags = set(flags)
    nc = bass.Bass("TRN2", target_bir_lowering=False)
    ctx = ExitStack()
    b = B(nc, ctx)
    dd = {}

    def din(name, shape, dt):
        dd[name] = nc.dram_tensor(name, list(shape), dt, kind="ExternalInput").ap()
        return dd[name]

    xt_full = din("xt_full", (D, T), F32)
    xt_own = din("xt_own", (D, TH), F32)
    wq_p = din("wq_p", (2048, 1024), F32)
    wkvd_p = din("wkvd_p", (128, 512), F32)
    wkup = din("wkup", (64, 2048), F32)
    wvup = din("wvup", (64, 2048), F32)
    wo_t = din("wo_t", (2048, 1024), F32)
    w1_p = din("w1_p", (4096, 1024), F32)
    w2_p = din("w2_p", (4096, 1024), F32)
    h1_p = din("h1_p", (4096, 4096), F32)
    h2_p = din("h2_p", (4096, 4096), F32)
    w3_p = din("w3_p", (1024, 4096), F32)
    s1_t = din("s1_t", (128, 32), F32)
    s2_t = din("s2_t", (128, 32), F32)
    h1s_t = din("h1s_t", (128, 32), F32)
    h2s_t = din("h2s_t", (128, 32), F32)
    w3s_t = din("w3s_t", (128, 8), F32)
    wq1_p = din("wq1_p", (1024, 1024), F32)
    wq1s_p = din("wq1s_p", (1024, 1024), F32)
    wk1_p = din("wk1_p", (1024, 1024), F32)
    wk1s_p = din("wk1s_p", (1024, 1024), F32)
    wv1_t = din("wv1_t", (1024, 1024), F32)
    wo1_t = din("wo1_t", (1024, 1024), F32)
    wg_p = din("wg_p", (3072, 1024), I32)
    wu_p = din("wu_p", (3072, 1024), I32)
    wd_p = din("wd_p", (1024, 3072), I32)
    ck_t = din("ck_t", (128, T), F32)
    sk_t = din("sk_t", (128, T), F32)
    cq_t = din("cq_t", (128, TH), F32)
    sq_t = din("sq_t", (128, TH), F32)
    ones_col_d = din("ones_col", (128, 1), BF16)
    ones_row_d = din("ones_row", (1, 128), BF16)

    outT = nc.dram_tensor("outT", [D, TH], F32, kind="ExternalOutput").ap()
    wsc = {}
    for nmm, shape in (("w1", (4096, 1024)), ("w2", (4096, 1024)),
                       ("h1", (4096, 4096)), ("h2", (4096, 4096)),
                       ("w3", (1024, 4096)),
                       ("q1", (1024, 1024)), ("q1s", (1024, 1024)),
                       ("k1", (1024, 1024)), ("k1s", (1024, 1024)),
                       ("v1", (1024, 1024)), ("o1", (1024, 1024)),
                       ("wg", (3072, 1024)), ("wu", (3072, 1024)),
                       ("wd", (1024, 3072))):
        wsc[nmm] = nc.dram_tensor("wsc_" + nmm, list(shape), BF16).ap()
    resd = nc.dram_tensor("resd", [D, TH], F32).ap()
    agin = nc.dram_tensor("agin", [D, TH], BF16)
    agout = nc.dram_tensor("agout", [2 * D, TH], BF16)
    if stop is not None:
        dbg_r = nc.dram_tensor("dbg_r", [D, TH], F32, kind="ExternalOutput").ap()
        dbg_x0 = nc.dram_tensor("dbg_x0", [128, 8 * TH], BF16, kind="ExternalOutput").ap()
        dbg_x1 = nc.dram_tensor("dbg_x1", [128, 8 * TH], BF16, kind="ExternalOutput").ap()

    with ctx:
        ocol = ctx.enter_context(nc.sbuf_tensor("ocol", [128, 1], BF16))
        orow = ctx.enter_context(nc.sbuf_tensor("orow", [1, 128], BF16))

        def ln_pipe(tag, src_ap, src_col0, sinks, src_waits, sl):
            W = TH
            lnstg = [sl.enter_context(nc.sbuf_tensor(tag + f"_stg{i}", [128, W], F32))
                     for i in range(2)]
            xbf = sl.enter_context(nc.sbuf_tensor(tag + "_xbf", [128, 8 * W], BF16))
            xsq = sl.enter_context(nc.sbuf_tensor(tag + "_xsq", [128, 8 * W], BF16))
            stt = sl.enter_context(nc.sbuf_tensor(tag + "_stt", [1, 3072], F32))
            sttb = sl.enter_context(nc.sbuf_tensor(tag + "_sttb", [1, 1024], BF16))
            rbsb = sl.enter_context(nc.sbuf_tensor(tag + "_rb", [128, W], BF16))
            mrbsb = sl.enter_context(nc.sbuf_tensor(tag + "_mrb", [128, W], BF16))
            psx = [sl.enter_context(nc.psum_tensor(tag + f"_px{i}", [1, 512], F32)) for i in range(2)]
            psv = [sl.enter_context(nc.psum_tensor(tag + f"_pv{i}", [1, 512], F32)) for i in range(2)]
            psb = [sl.enter_context(nc.psum_tensor(tag + f"_pb{i}", [128, 1024], F32)) for i in range(2)]
            nb = W // 512
            v_x = [0] * 8
            a_sq = [0] * 8
            for dc in range(8):
                lsem = b.sem(f"ln{dc % 2}")
                d_x = b.dma("scalar", lnstg[dc % 2][:, :],
                            src_ap[dc * 128:(dc + 1) * 128, src_col0:src_col0 + W],
                            waits=list(src_waits) + ([(b.tV, v_x[dc - 2])]
                                                     if dc >= 2 else []),
                            sem=lsem)
                v_x[dc] = b.vcopy(xbf[:, dc * W:(dc + 1) * W], lnstg[dc % 2][:, :],
                                  waits=[(lsem, d_x)])
            for dc in range(8):
                a_sq[dc] = b.act(xsq[:, dc * W:(dc + 1) * W], xbf[:, dc * W:(dc + 1) * W],
                                 AF.Square, waits=[(b.tV, v_x[dc])])
            a_bc = [0] * nb
            v_fin = [0] * nb
            a_sq_done = 0
            t_bc_prev = 0
            for tb in range(nb):
                px, pv = psx[tb % 2], psv[tb % 2]
                w = [(b.tA, a_sq[7]), (b.dS, b.d_ones)]
                if tb >= 2:
                    w.append((b.tV, v_fin[tb - 2]))
                for dc in range(8):
                    s0 = slice(dc * W + tb * 512, dc * W + (tb + 1) * 512)
                    b.mm(px[0:1, :], ocol[:, 0:1], xbf[:, s0],
                         start=(dc == 0), stop=(dc == 7), waits=(w if dc == 0 else ()))
                    t = b.mm(pv[0:1, :], ocol[:, 0:1], xsq[:, s0],
                             start=(dc == 0), stop=(dc == 7), inc=(dc == 7))
                b.vcopy(stt[0:1, 0:512], px[0:1, :], waits=[(b.tT, t), (b.tA, a_sq_done)],
                        inc=False)
                b.vcopy(stt[0:1, 512:1024], pv[0:1, :], inc=False)
                v1 = b.ts(stt[0:1, 1024:2048], stt[0:1, 0:1024], 1.0 / D, None, ALU.mult)
                a1 = b.act(stt[0:1, 2048:2560], stt[0:1, 1024:1536], AF.Square,
                           waits=[(b.tV, v1)])
                b.tt(stt[0:1, 1536:2048], stt[0:1, 1536:2048], stt[0:1, 2048:2560],
                     ALU.subtract, waits=[(b.tA, a1)], inc=False)
                v2 = b.ts(stt[0:1, 1536:2048], stt[0:1, 1536:2048], EPS, None, ALU.add)
                a2 = b.act(stt[0:1, 1536:2048], stt[0:1, 1536:2048], AF.Sqrt,
                           waits=[(b.tV, v2)])
                a_sq_done = a2
                b.vrecip(stt[0:1, 1536:2048], stt[0:1, 1536:2048], waits=[(b.tA, a2)],
                         inc=False)
                b.tt(stt[0:1, 2560:3072], stt[0:1, 1024:1536], stt[0:1, 1536:2048],
                     ALU.mult, inc=False)
                b.vcopy(sttb[0:1, 0:512], stt[0:1, 1536:2048],
                        waits=[(b.tT, t_bc_prev)], inc=False)
                v6 = b.vcopy(sttb[0:1, 512:1024], stt[0:1, 2560:3072])
                pb = psb[tb % 2]
                w = [(b.tV, v6), (b.dS, b.d_ones)]
                if tb >= 2:
                    w.append((b.tA, a_bc[tb - 2]))
                b.mm(pb[:, 0:512], orow[0:1, :], sttb[0:1, 0:512],
                     start=True, stop=True, waits=w)
                t2 = b.mm(pb[:, 512:1024], orow[0:1, :], sttb[0:1, 512:1024],
                          start=True, stop=True, inc=True)
                t_bc_prev = t2
                b.scopy(rbsb[:, tb * 512:(tb + 1) * 512], pb[:, 0:512], waits=[(b.tT, t2)])
                a_bc[tb] = b.scopy(mrbsb[:, tb * 512:(tb + 1) * 512], pb[:, 512:1024])
                v_fin[tb] = v6
            vlast = 0
            for dc in range(8):
                for (xln, dcol0) in sinks:
                    dsl = slice(dc * TH + dcol0, dc * TH + dcol0 + W)
                    b.tt(xln[:, dsl], xbf[:, dc * W:(dc + 1) * W],
                         rbsb[:, :], ALU.mult,
                         waits=[(b.tA, a_bc[nb - 1])], inc=False)
                    vlast = b.tt(xln[:, dsl], xln[:, dsl], mrbsb[:, :], ALU.subtract)
            return vlast

        scopeA = ExitStack()
        xln_b0 = scopeA.enter_context(nc.sbuf_tensor("xln_b0", [128, 8 * TH], BF16))
        xln_b1 = scopeA.enter_context(nc.sbuf_tensor("xln_b1", [128, 8 * TH], BF16))
        d_init = 0
        for blk in range(2):
            with ExitStack() as s0:
                b.phase()
                if blk == 0:
                    b.dma("scalar", ocol[:, :], ones_col_d[:, :])
                    b.d_ones = b.dma("scalar", orow[:, :], ones_row_d[:, :])
                    d_init = b.dma("gpsimd", resd[:, :], xt_own[:, :])
                ln_pipe(f"p0{blk}", xt_full, blk * TH,
                        [((xln_b0, xln_b1)[blk], 0)], (), s0)
                b.endphase()
        b.resd_w = [(b.dG, d_init)]

        def dump_stop(x0=None, x1=None):
            b.phase()
            b.dma("gpsimd", dbg_r[:, :], resd[:, :], waits=list(b.resd_w))
            if x0 is not None:
                for dc in range(8):
                    b.dma("gpsimd", dbg_x0[:, dc * TH:(dc + 1) * TH],
                          x0[:, dc * TH:(dc + 1) * TH])
            if x1 is not None:
                for dc in range(8):
                    b.dma("gpsimd", dbg_x1[:, dc * TH:(dc + 1) * TH],
                          x1[:, dc * TH:(dc + 1) * TH])
            b.ph.wait("sync", [(b.dG, b.dG.n), (b.dS, b.dS.n), (b.dT, b.dT.n)])
            b.endphase()

        if stop == "p0":
            dump_stop(xln_b0, xln_b1)
            scopeA.close()
            return nc, b.ninstr

        with ExitStack() as s1:
            kvlat = s1.enter_context(nc.sbuf_tensor("p1_kvlat", [64, T], BF16))
            q_sb = s1.enter_context(nc.sbuf_tensor("p1_q", [128, 2 * TH], BF16))
            k_sb = s1.enter_context(nc.sbuf_tensor("p1_k", [128, 2 * T], BF16))
            v_sb = s1.enter_context(nc.sbuf_tensor("p1_v", [128, 32 * 256], BF16))
            Pb = [s1.enter_context(nc.sbuf_tensor(f"p1_P{i}", [128, T], BF16)) for i in range(2)]
            PTb = [s1.enter_context(nc.sbuf_tensor(f"p1_PT{i}", [128, T], BF16,
                                                   is_transpose_dest=True)) for i in range(2)]
            obufA = s1.enter_context(nc.sbuf_tensor("p1_oA", [128, 2048], BF16))
            obufB = s1.enter_context(nc.sbuf_tensor("p1_oB", [128, 2048], BF16))
            otb = s1.enter_context(nc.sbuf_tensor("p1_ot", [128, 2 * 4096], BF16,
                                                  is_transpose_dest=True))
            wqb = [s1.enter_context(nc.sbuf_tensor(f"p1_wq{i}", [128, 2048], BF16)) for i in range(2)]
            wkvdb = s1.enter_context(nc.sbuf_tensor("p1_wkvd", [128, 512], BF16))
            wkupb = s1.enter_context(nc.sbuf_tensor("p1_wkup", [64, 2048], BF16))
            wvupb = s1.enter_context(nc.sbuf_tensor("p1_wvup", [64, 2048], BF16))
            wob = s1.enter_context(nc.sbuf_tensor("p1_wo", [128, 4096], BF16))
            negb = s1.enter_context(nc.sbuf_tensor("p1_negb", [128, 2], F32))
            negb2 = s1.enter_context(nc.sbuf_tensor("p1_negb2", [128, 2], F32))
            lpart = s1.enter_context(nc.sbuf_tensor("p1_lpart", [128, 8], F32))
            lsum = s1.enter_context(nc.sbuf_tensor("p1_lsum", [128, 2], F32))
            rlb = s1.enter_context(nc.sbuf_tensor("p1_rl", [128, 2], F32))
            optmp = [s1.enter_context(nc.sbuf_tensor(f"p1_opt{i}", [128, 512], F32)) for i in range(4)]
            rescho = [s1.enter_context(nc.sbuf_tensor(f"p1_rco{i}", [128, 512], F32)) for i in range(2)]
            psA = s1.enter_context(nc.psum_tensor("p1_psA", [128, 1024], F32))
            psB = s1.enter_context(nc.psum_tensor("p1_psB", [128, 1024], F32))
            psSub = s1.enter_context(nc.psum_tensor("p1_psSub", [128, 512], F32))
            psO = s1.enter_context(nc.psum_tensor("p1_psO", [128, 256], F32))
            psE1 = s1.enter_context(nc.psum_tensor("p1_psE1", [128, 512], F32))
            psE2 = s1.enter_context(nc.psum_tensor("p1_psE2", [128, 512], F32))

            b.phase()
            d_wkvd = b.dma("gpsimd", wkvdb[:, :], wkvd_p[:, :])
            d_wkup = b.dma("gpsimd", wkupb[:, :], wkup[:, :])
            d_wvup = b.dma("gpsimd", wvupb[:, :], wvup[:, :])
            d_wq = [[0, 0] for _ in range(H0)]
            for oc2 in range(2):
                d_wq[0][oc2] = b.dma("gpsimd", wqb[0][:, oc2 * 1024:(oc2 + 1) * 1024],
                                     wq_p[oc2 * 128:(oc2 + 1) * 128, :])

            a_kv = 0
            for tb in range(8):
                pe = (psE1, psE2)[tb % 2]
                w = [(b.dG, d_wkvd)]
                if tb >= 2:
                    w.append((b.tA, a_kv))
                xln = (xln_b0, xln_b1)[tb // 4]
                for dc in range(8):
                    t = b.mm(pe[0:64, :], wkvdb[:, dc * 64:(dc + 1) * 64],
                             xln[:, dc * TH + (tb % 4) * 512: dc * TH + (tb % 4) * 512 + 512],
                             start=(dc == 0), stop=(dc == 7),
                             waits=(w if dc == 0 else ()), inc=(dc == 7))
                a_kv = b.scopy(kvlat[:, tb * 512:(tb + 1) * 512], pe[0:64, :],
                               waits=[(b.tT, t)])

            ring_exp = [0, 0]
            ring_P = [0, 0]
            ring_PT = [0, 0]
            v_negb_cur = [0, 0]
            v_rl_cur = [0, 0]
            a_exp_last = [0, 0]
            v_Oev_last = 0
            v_op_last = 0
            t_op_last = 0
            d_otr_last = 0
            nop = 0
            st_hist = [None] * 32
            st_ring = [(b.dS, 0)] * 4
            v_rcread = [0, 0]
            d_Otr = [0] * H0

            for h in range(H0):
                par_h = h % 2
                hs_tA = b.tA.n
                hs_tV = b.tV.n
                if h + 1 < H0:
                    for oc2 in range(2):
                        d_wq[h + 1][oc2] = b.dma(
                            "gpsimd", wqb[(h + 1) % 2][:, oc2 * 1024:(oc2 + 1) * 1024],
                            wq_p[(2 * (h + 1) + oc2) * 128:(2 * (h + 1) + oc2 + 1) * 128, :],
                            waits=[(b.tT, b.tT.n)] if h >= 1 else ())
                a_q = 0
                for oc2 in range(2):
                    for tb in range(4):
                        i = oc2 * 4 + tb
                        pe = (psE1, psE2)[i % 2]
                        w = [(b.dG, d_wq[h][oc2])]
                        if i >= 2:
                            w.append((b.tA, a_q))
                        else:
                            w += [(b.tA, hs_tA), (b.tV, hs_tV)]
                        for ic in range(8):
                            tq = b.mm(pe[:, :],
                                      wqb[par_h][:, oc2 * 1024 + ic * 128: oc2 * 1024 + (ic + 1) * 128],
                                      xln_b0[:, ic * TH + tb * 512: ic * TH + (tb + 1) * 512],
                                      start=(ic == 0), stop=(ic == 7),
                                      waits=(w if ic == 0 else ()), inc=(ic == 7))
                        a_q = b.smul(q_sb[:, oc2 * TH + tb * 512: oc2 * TH + (tb + 1) * 512],
                                     pe[:, :], SC0, waits=[(b.tT, tq)])
                a_k = a_q
                for oc2 in range(2):
                    for tb in range(8):
                        i = oc2 * 8 + tb
                        pe = (psE1, psE2)[i % 2]
                        w = [(b.dG, d_wkup), (b.tA, a_kv)]
                        w.append((b.tA, a_k))
                        tk = b.mm(pe[:, :], wkupb[:, h * 256 + oc2 * 128: h * 256 + (oc2 + 1) * 128],
                                  kvlat[:, tb * 512:(tb + 1) * 512],
                                  start=True, stop=True, waits=w, inc=True)
                        a_k = b.scopy(k_sb[:, oc2 * T + tb * 512: oc2 * T + (tb + 1) * 512],
                                      pe[:, :], waits=[(b.tT, tk)])
                v_V = 0
                for tc in range(32):
                    pe = (psE1, psE2)[tc % 2]
                    w = [(b.dG, d_wvup), (b.tA, a_kv), (b.tA, a_k)]
                    if tc >= 2:
                        w.append((b.tV, v_V))
                    tv = b.mm(pe[:, 0:256], kvlat[:, tc * 128:(tc + 1) * 128],
                              wvupb[:, h * 256:(h + 1) * 256],
                              start=True, stop=True, waits=w, inc=True)
                    v_V = b.vcopy(v_sb[:, tc * 256:(tc + 1) * 256], pe[:, 0:256],
                                  waits=[(b.tT, tv)])

                if stop == "p1a" and h == 0:
                    b.ph.wait("gpsimd", [(b.tA, a_q), (b.tA, a_k), (b.tV, v_V)])
                    b.dma("gpsimd", dbg_x0[0:64, 0:T], kvlat[:, :])
                    b.dma("gpsimd", dbg_x0[:, 4096:8192], q_sb[:, :])
                    b.dma("gpsimd", dbg_x1[:, 0:8192], k_sb[:, :])
                    b.dma("gpsimd", dbg_x1[:, 8192:16384], v_sb[:, :])
                    b.ph.wait("sync", [(b.dG, b.dG.n)])
                    b.endphase()
                    s1.close()
                    scopeA.close()
                    return nc, b.ninstr

                for qt in range(16):
                    par = qt % 2
                    w = [(b.tV, v_negb_cur[0]), (b.tV, v_negb_cur[1])]
                    if qt == 0:
                        w += [(b.tA, a_k), (b.tA, a_q)]
                    for dc2 in range(2):
                        t_sub = b.mm(psSub[:, :],
                                     q_sb[:, dc2 * TH + qt * 128: dc2 * TH + (qt + 1) * 128],
                                     k_sb[:, dc2 * T:(dc2 + 1) * T:8],
                                     start=(dc2 == 0), stop=(dc2 == 1),
                                     waits=(w if dc2 == 0 else ()), inc=(dc2 == 1))
                    vnb = b.vred(negb[:, par:par + 1], psSub[:, :], ALU.max, negate=True,
                                 waits=[(b.tT, t_sub), (b.tA, a_exp_last[par])])
                    b.ph.wait("vector", [(b.tV, vnb)])
                    v_negb_cur[par] = b.ts(negb2[:, par:par + 1], negb[:, par:par + 1],
                                           -MARGIN, None, ALU.add)
                    aes = [0] * 4
                    for kbi in range(4):
                        psX = (psA, psB)[kbi % 2]
                        w = [(b.tA, ring_exp[kbi % 2])]
                        for dc2 in range(2):
                            for hf in range(2):
                                tS = b.mm(psX[:, hf * 512:(hf + 1) * 512],
                                          q_sb[:, dc2 * TH + qt * 128: dc2 * TH + (qt + 1) * 128],
                                          k_sb[:, dc2 * T + kbi * 1024 + hf * 512:
                                               dc2 * T + kbi * 1024 + (hf + 1) * 512],
                                          start=(dc2 == 0), stop=(dc2 == 1),
                                          waits=(w if (dc2 == 0 and hf == 0) else ()),
                                          inc=(dc2 == 1 and hf == 1))
                        we = [(b.tT, tS), (b.tV, v_negb_cur[par]),
                              (b.dT, ring_P[par]), (b.tV, v_rl_cur[par])]
                        ae = b.act(Pb[par][:, kbi * 1024:(kbi + 1) * 1024], psX[:, :],
                                   AF.Exp, bias=negb2[:, par:par + 1], scale=1.0,
                                   accum=lpart[:, 4 * par + kbi: 4 * par + kbi + 1],
                                   waits=we)
                        aes[kbi] = ae
                        ring_exp[kbi % 2] = ae
                    a_exp_last[par] = aes[3]
                    ring_P[par] = b.tr(PTb[par][:].rearrange("p (kc q) -> p kc q", kc=32),
                                       Pb[par][:],
                                       waits=[(b.tA, aes[3]), (b.tT, ring_PT[par])])
                    vls = b.vred(lsum[:, par:par + 1], lpart[:, 4 * par:4 * par + 4],
                                 ALU.add, waits=[(b.tA, aes[3])])
                    b.ph.wait("vector", [(b.tV, vls)])
                    v_rl_cur[par] = b.vrecip(rlb[:, par:par + 1], lsum[:, par:par + 1])
                    wo_ = [(b.dT, ring_P[par]), (b.tV, v_Oev_last), (b.tV, v_V)]
                    for kc in range(32):
                        tO = b.mm(psO[:, :], PTb[par][:, kc * 128:(kc + 1) * 128],
                                  v_sb[:, kc * 256:(kc + 1) * 256],
                                  start=(kc == 0), stop=(kc == 31),
                                  waits=(wo_ if kc == 0 else ()), inc=(kc == 31))
                    ring_PT[par] = tO
                    b.ph.wait("vector", [(b.tT, tO), (b.tV, v_rl_cur[par]),
                                         (b.dT, d_otr_last)])
                    b.ts(obufA[:, qt * 128:(qt + 1) * 128], psO[:, 0:128],
                         rlb[:, par:par + 1], None, ALU.mult, inc=False)
                    v_Oev_last = b.ts(obufB[:, qt * 128:(qt + 1) * 128], psO[:, 128:256],
                                      rlb[:, par:par + 1], None, ALU.mult)
                if stop == "p1b" and h == 0:
                    b.ph.wait("gpsimd", [(b.tV, v_Oev_last), (b.tA, a_exp_last[1]),
                                         (b.dT, ring_P[1])])
                    b.dma("gpsimd", dbg_x0[:, 0:2048], obufA[:, :])
                    b.dma("gpsimd", dbg_x0[:, 2048:4096], obufB[:, :])
                    b.dma("gpsimd", dbg_x0[:, 4096:8192], Pb[1][:, :])
                    b.dma("gpsimd", dbg_x0[:, 8192:12288], PTb[1][:, :])
                    b.dma("gpsimd", dbg_x0[:, 12288:12296], lpart[:, :])
                    b.dma("gpsimd", dbg_x0[:, 12296:12298], negb2[:, :])
                    b.dma("gpsimd", dbg_x0[:, 12304:12306], rlb[:, :])
                    b.dma("gpsimd", dbg_x0[:, 12306:12308], lsum[:, :])
                    b.dma("gpsimd", dbg_x1[:, 0:8192], v_sb[:, :])
                    b.ph.wait("sync", [(b.dG, b.dG.n)])
                    b.endphase()
                    s1.close()
                    scopeA.close()
                    return nc, b.ninstr

                wtr = [(b.tV, v_Oev_last), (b.tT, t_op_last)]
                b.tr(otb[:, par_h * 4096: par_h * 4096 + 2048].rearrange("p (t q) -> p t q", t=16),
                     obufA[:, :], waits=wtr)
                d_Otr[h] = b.tr(
                    otb[:, par_h * 4096 + 2048: par_h * 4096 + 4096].rearrange("p (t q) -> p t q", t=16),
                    obufB[:, :])
                d_otr_last = d_Otr[h]

                if h % 2 == 1:
                    g = h // 2
                    d_wo = 0
                    for s in range(4):
                        d_wo = b.dma("gpsimd", wob[:, s * 1024:(s + 1) * 1024],
                                     wo_t[g * 512 + s * 128: g * 512 + (s + 1) * 128, :],
                                     waits=[(b.tT, t_op_last)] if s == 0 else ())
                    v_op_prev = [b.tV.n, b.tV.n]
                    for dc in range(8):
                        for tb in range(4):
                            i = dc * 4 + tb
                            pp = (psE1, psE2)[i % 2]
                            tile = resd[dc * 128:(dc + 1) * 128, tb * 512:(tb + 1) * 512]
                            rsem = b.sem(f"rc{i % 2}")
                            ssem = b.sem(f"so{i % 2}")
                            wrc = [(b.tV, v_rcread[i % 2])]
                            wrc += (list(b.resd_w) if st_hist[i] is None
                                    else [st_hist[i]])
                            d_rc2 = b.dma("scalar", rescho[i % 2][:, :], tile, waits=wrc,
                                          sem=rsem)
                            w = [(b.dG, d_wo), (b.dT, d_Otr[h]), (b.dT, d_Otr[h - 1]),
                                 (b.tV, v_op_prev[i % 2]), (b.tA, b.tA.n if i < 2 else 0)]
                            for cc in range(4):
                                m, dc2 = cc // 2, cc % 2
                                top = b.mm(pp[:, :],
                                           wob[:, cc * 1024 + dc * 128: cc * 1024 + (dc + 1) * 128],
                                           otb[:, m * 4096 + dc2 * 2048 + tb * 512:
                                               m * 4096 + dc2 * 2048 + (tb + 1) * 512],
                                           start=(cc == 0), stop=(cc == 3),
                                           waits=(w if cc == 0 else ()), inc=(cc == 3))
                            vop = b.tt(optmp[nop % 4][:, :], rescho[i % 2][:, :], pp[:, :],
                                       ALU.add,
                                       waits=[(b.tT, top), (rsem, d_rc2),
                                              st_ring[nop % 4]])
                            v_op_prev[i % 2] = vop
                            v_rcread[i % 2] = vop
                            st_hist[i] = (ssem, b.dma("scalar", tile, optmp[nop % 4][:, :],
                                                      waits=[(b.tV, vop)], sem=ssem))
                            st_ring[nop % 4] = st_hist[i]
                            nop += 1
                    v_op_last = vop
                    t_op_last = top
            b.resd_w = [(b.sem("so0"), b.sem("so0").n),
                        (b.sem("so1"), b.sem("so1").n)]
            b.endphase()
        if stop == "p1":
            dump_stop(xln_b0, xln_b1)
            scopeA.close()
            return nc, b.ninstr
        scopeA.close()

        scopeB = ExitStack()
        xln_own = scopeB.enter_context(nc.sbuf_tensor("xln_own", [128, 8 * TH], BF16))

        def ln_phase(tag, dst):
            with ExitStack() as sl:
                b.phase()
                ln_pipe(tag, resd, 0, [(dst, 0)], b.resd_w, sl)
                b.endphase()

        ln_phase("p2", xln_own)
        if stop == "p2":
            dump_stop(xln_own)
            scopeB.close()
            return nc, b.ninstr

        with ExitStack() as sc2:
            stgf = [sc2.enter_context(nc.sbuf_tensor(f"pc_f{i}", [128, 4096], F32))
                    for i in range(3)]
            stgi = [sc2.enter_context(nc.sbuf_tensor(f"pc_i{i}", [128, 3072], I32))
                    for i in range(2)]
            stgb = [sc2.enter_context(nc.sbuf_tensor(f"pc_b{i}", [128, 4096], BF16))
                    for i in range(3)]
            b.phase()
            jobs = [(w1_p, wsc["w1"], 4096, 1024, 0), (w2_p, wsc["w2"], 4096, 1024, 0),
                    (h1_p, wsc["h1"], 4096, 4096, 0), (h2_p, wsc["h2"], 4096, 4096, 0),
                    (w3_p, wsc["w3"], 1024, 4096, 0),
                    (wq1_p, wsc["q1"], 1024, 1024, 0), (wq1s_p, wsc["q1s"], 1024, 1024, 0),
                    (wk1_p, wsc["k1"], 1024, 1024, 0), (wk1s_p, wsc["k1s"], 1024, 1024, 0),
                    (wv1_t, wsc["v1"], 1024, 1024, 0), (wo1_t, wsc["o1"], 1024, 1024, 0),
                    (wg_p, wsc["wg"], 3072, 1024, 1), (wu_p, wsc["wu"], 3072, 1024, 1),
                    (wd_p, wsc["wd"], 1024, 3072, 1)]
            nf = ni_ = 0
            ld_hist = {}
            cast_hist = {}
            st_hist = {}
            for (srcd, dstd, rows, wdt, isint) in jobs:
                for r0 in range(0, rows, 128):
                    if isint:
                        slot = ni_ % 2
                        stg = stgi[slot][:, 0:wdt]
                        key = ("i", slot)
                        ni_ += 1
                    else:
                        slot = nf % 3
                        stg = stgf[slot][:, 0:wdt]
                        key = ("f", slot)
                        nf += 1
                    bslot = (nf + ni_) % 3
                    lsem = b.sem("cv_%s%d" % key)
                    ssem = b.sem(f"cs{bslot}")
                    dl = b.dma("scalar", stg, srcd[r0:r0 + 128, 0:wdt],
                               waits=[(b.tV, cast_hist.get(key, 0))], sem=lsem)
                    vc = b.vcopy(stgb[bslot][:, 0:wdt], stg,
                                 waits=[(lsem, dl), (ssem, st_hist.get(bslot, 0))])
                    cast_hist[key] = vc
                    st_hist[bslot] = b.dma("sync", dstd[r0:r0 + 128, 0:wdt],
                                           stgb[bslot][:, 0:wdt], waits=[(b.tV, vc)],
                                           sem=ssem)
            b.ph.wait("gpsimd", [(b.sem(f"cs{i}"), b.sem(f"cs{i}").n) for i in range(3)])
            b.endphase()

        with ExitStack() as s3:
            a_sb = s3.enter_context(nc.sbuf_tensor("p3_a", [128, 32 * 1024], BF16))
            g_sb = s3.enter_context(nc.sbuf_tensor("p3_g", [128, 32 * 1024], BF16))
            wblk = [s3.enter_context(nc.sbuf_tensor(f"p3_w{i}", [128, 4096], BF16)) for i in range(2)]
            scl = s3.enter_context(nc.sbuf_tensor("p3_scl", [128, 160], F32))
            tmpb = [s3.enter_context(nc.sbuf_tensor(f"p3_tb{i}", [128, 1024], BF16)) for i in range(2)]
            tmpf = [s3.enter_context(nc.sbuf_tensor(f"p3_tf{i}", [128, 1024], F32)) for i in range(2)]
            resch = [s3.enter_context(nc.sbuf_tensor(f"p3_rc{i}", [128, 1024], F32)) for i in range(2)]
            psF = [s3.enter_context(nc.psum_tensor(f"p3_ps{i}", [128, 1024], F32)) for i in range(2)]

            b.phase()
            d_scl = 0
            for i, sd in enumerate((s1_t, s2_t, h1s_t, h2s_t, w3s_t)):
                wdt = 32 if i < 4 else 8
                d_scl = b.dma("scalar", scl[:, i * 32: i * 32 + wdt], sd[:, :],
                              sem=b.sem("scl"))
            t_blkfree = [0, 0]
            resd_done = 0

            def mat3(src, nic, noc, rhs_of, evac, first_extra=()):
                ev_prev = [(b.tA, b.tA.n), (b.tV, b.tV.n)]
                for oc in range(noc):
                    slot = oc % 2
                    wsem = b.sem(f"wb{slot}")
                    dB = b.dma("scalar", wblk[slot][:, 0:nic * 128],
                               src[oc * 128:(oc + 1) * 128, 0:nic * 128],
                               waits=[(b.tT, t_blkfree[slot])], sem=wsem)
                    pf = psF[oc % 2]
                    w = [(wsem, dB)] + (list(first_extra) if oc == 0 else [])
                    if oc >= 2:
                        w.append(ev_prev[oc % 2])
                    else:
                        w += [(b.tA, b.tA.n), (b.tV, b.tV.n)]
                    for ic in range(nic):
                        for hf in range(2):
                            tm = b.mm(pf[:, hf * 512:(hf + 1) * 512],
                                      wblk[slot][:, ic * 128:(ic + 1) * 128],
                                      rhs_of(ic, hf),
                                      start=(ic == 0), stop=(ic == nic - 1),
                                      waits=(w if (ic == 0 and hf == 0) else ()),
                                      inc=(ic == nic - 1 and hf == 1))
                    t_blkfree[slot] = tm
                    ev_prev[oc % 2] = evac(oc, pf, tm)

            for p in range(2):
                tok = slice(p * 1024, (p + 1) * 1024)

                def x_rhs(ic, hf, p=p):
                    return xln_own[:, ic * TH + p * 1024 + hf * 512:
                                   ic * TH + p * 1024 + (hf + 1) * 512]

                def a_rhs(ic, hf):
                    return a_sb[:, ic * 1024 + hf * 512: ic * 1024 + (hf + 1) * 512]

                def g_rhs(ic, hf):
                    return g_sb[:, ic * 1024 + hf * 512: ic * 1024 + (hf + 1) * 512]

                def ev_t1(oc, pf, tm):
                    a = b.act(a_sb[:, oc * 1024:(oc + 1) * 1024], pf[:, :], AF.Silu,
                              scale=scl[:, oc:oc + 1], waits=[(b.tT, tm), (b.sem("scl"), d_scl)])
                    return (b.tA, a)
                mat3(wsc["w1"], 8, 32, x_rhs, ev_t1)
                a_t1 = b.tA.n

                def ev_t2(oc, pf, tm):
                    b.ts(tmpb[oc % 2][:, :], pf[:, :], scl[:, 32 + oc: 33 + oc], None,
                         ALU.mult, waits=[(b.tT, tm), (b.tA, a_t1), (b.sem("scl"), d_scl)], inc=False)
                    v2 = b.tt(a_sb[:, oc * 1024:(oc + 1) * 1024],
                              a_sb[:, oc * 1024:(oc + 1) * 1024], tmpb[oc % 2][:, :], ALU.mult)
                    return (b.tV, v2)
                mat3(wsc["w2"], 8, 32, x_rhs, ev_t2)
                v_t2 = b.tV.n

                def ev_y1(oc, pf, tm):
                    a = b.act(g_sb[:, oc * 1024:(oc + 1) * 1024], pf[:, :], AF.Gelu,
                              scale=scl[:, 64 + oc: 65 + oc], waits=[(b.tT, tm)])
                    return (b.tA, a)
                mat3(wsc["h1"], 32, 32, a_rhs, ev_y1, first_extra=[(b.tV, v_t2)])
                t_y1 = b.tT.n

                def ev_y2(oc, pf, tm):
                    a = b.act(a_sb[:, oc * 1024:(oc + 1) * 1024], pf[:, :], AF.Gelu,
                              scale=scl[:, 96 + oc: 97 + oc], waits=[(b.tT, tm)])
                    return (b.tA, a)
                mat3(wsc["h2"], 32, 32, g_rhs, ev_y2, first_extra=[(b.tT, t_y1)])
                t_y2 = b.tT.n

                d_rc = [0, 0]
                d_st = [0, 0]
                v_rc = [b.tV.n, b.tV.n]

                def ev_w3(oc, pf, tm, p=p, tok=tok):
                    sl2 = oc % 2
                    b.ts(tmpf[sl2][:, :], pf[:, :], scl[:, 128 + oc: 129 + oc], None,
                         ALU.mult,
                         waits=[(b.tT, tm), (b.sem(f"rc{sl2}"), d_rc[sl2])], inc=False)
                    v2 = b.tt(resch[sl2][:, :], resch[sl2][:, :], tmpf[sl2][:, :], ALU.add)
                    d_st[sl2] = b.dma("scalar", resd[oc * 128:(oc + 1) * 128, tok],
                                      resch[sl2][:, :], waits=[(b.tV, v2)],
                                      sem=b.sem(f"so{sl2}"))
                    v_rc[sl2] = v2
                    return (b.tV, v2)

                ev_prev = [(b.tA, b.tA.n), (b.tV, b.tV.n)]
                for oc in range(8):
                    sl2 = oc % 2
                    d_rc[sl2] = b.dma("scalar", resch[sl2][:, :],
                                      resd[oc * 128:(oc + 1) * 128, tok],
                                      waits=list(b.resd_w) + [(b.tV, v_rc[sl2]),
                                                              (b.sem(f"so{sl2}"), d_st[sl2])],
                                      sem=b.sem(f"rc{sl2}"))
                    slot = oc % 2
                    wsem = b.sem(f"wb{slot}")
                    dB = b.dma("scalar", wblk[slot][:, :],
                               wsc["w3"][oc * 128:(oc + 1) * 128, :],
                               waits=[(b.tT, t_blkfree[slot])], sem=wsem)
                    pf = psF[oc % 2]
                    w = [(wsem, dB), (b.tT, t_y2), ev_prev[oc % 2]]
                    for ic in range(32):
                        for hf in range(2):
                            tm = b.mm(pf[:, hf * 512:(hf + 1) * 512],
                                      wblk[slot][:, ic * 128:(ic + 1) * 128],
                                      a_rhs(ic, hf),
                                      start=(ic == 0), stop=(ic == 31),
                                      waits=(w if (ic == 0 and hf == 0) else ()),
                                      inc=(ic == 31 and hf == 1))
                    t_blkfree[slot] = tm
                    ev_prev[oc % 2] = ev_w3(oc, pf, tm)
            b.resd_w = [(b.sem("so0"), b.sem("so0").n),
                        (b.sem("so1"), b.sem("so1").n)]
            b.endphase()
        if stop == "p3":
            dump_stop()
            scopeB.close()
            return nc, b.ninstr

        scopeC = ExitStack()
        xln_c0 = scopeC.enter_context(nc.sbuf_tensor("c_xb0", [128, 8 * TH], BF16))
        xln_c1 = scopeC.enter_context(nc.sbuf_tensor("c_xb1", [128, 8 * TH], BF16))
        ln_phase("p4", xln_own)

        b.phase()
        d_ag = 0
        for dc in range(8):
            d_ag = b.dma("scalar", agin.ap()[dc * 128:(dc + 1) * 128, :],
                         xln_own[:, dc * TH:(dc + 1) * TH], sem=b.sem("agi"))
        b.coll("AllGather", ALU.bypass, [[0, 1], [2, 3], [4, 5], [6, 7]],
               [agin.ap().opt()], [agout.ap().opt()], waits=[(b.sem("agi"), d_ag)])
        d_ago = 0
        for blk in range(2):
            dstb = (xln_c0, xln_c1)[blk]
            for dc in range(8):
                d_ago = b.dma("scalar", dstb[:, dc * TH:(dc + 1) * TH],
                              agout.ap()[blk * D + dc * 128: blk * D + (dc + 1) * 128, :],
                              waits=[(b.cC, b.cC.n)], sem=b.sem("ago"))
        b.endphase()
        if stop == "p4":
            b.phase()
            b.ph.wait("gpsimd", [(b.sem("ago"), d_ago)])
            b.endphase()
            dump_stop(xln_c0, xln_c1)
            scopeC.close()
            scopeB.close()
            return nc, b.ninstr

        with ExitStack() as s5:
            q2 = s5.enter_context(nc.sbuf_tensor("p5_q2", [128, TH], BF16))
            k2 = s5.enter_context(nc.sbuf_tensor("p5_k2", [128, T], BF16))
            v_sb = s5.enter_context(nc.sbuf_tensor("p5_v", [128, 32 * 256], BF16))
            Pb = [s5.enter_context(nc.sbuf_tensor(f"p5_P{i}", [128, T], BF16)) for i in range(2)]
            PTb = [s5.enter_context(nc.sbuf_tensor(f"p5_PT{i}", [128, T], BF16,
                                                   is_transpose_dest=True)) for i in range(2)]
            o2 = s5.enter_context(nc.sbuf_tensor("p5_o2", [128, 2048], BF16))
            otb = s5.enter_context(nc.sbuf_tensor("p5_ot", [128, 2048], BF16,
                                                  is_transpose_dest=True))
            wqk = [s5.enter_context(nc.sbuf_tensor(f"p5_wqk{i}", [128, 4096], BF16)) for i in range(2)]
            wvb = s5.enter_context(nc.sbuf_tensor("p5_wv", [128, 2048], BF16))
            wo1b = s5.enter_context(nc.sbuf_tensor("p5_wo1", [128, 1024], BF16))
            ctab = [s5.enter_context(nc.sbuf_tensor(f"p5_ct{i}", [128, 512], F32)) for i in range(2)]
            stab = [s5.enter_context(nc.sbuf_tensor(f"p5_st{i}", [128, 512], F32)) for i in range(2)]
            rtmp = [s5.enter_context(nc.sbuf_tensor(f"p5_rt{i}", [128, 512], F32)) for i in range(2)]
            negb = s5.enter_context(nc.sbuf_tensor("p5_negb", [128, 2], F32))
            negb2 = s5.enter_context(nc.sbuf_tensor("p5_negb2", [128, 2], F32))
            lpart = s5.enter_context(nc.sbuf_tensor("p5_lpart", [128, 8], F32))
            lsum = s5.enter_context(nc.sbuf_tensor("p5_lsum", [128, 2], F32))
            rlb = s5.enter_context(nc.sbuf_tensor("p5_rl", [128, 2], F32))
            optmp = [s5.enter_context(nc.sbuf_tensor(f"p5_opt{i}", [128, 512], F32)) for i in range(2)]
            rescho = [s5.enter_context(nc.sbuf_tensor(f"p5_rco{i}", [128, 512], F32)) for i in range(2)]
            psA = s5.enter_context(nc.psum_tensor("p5_psA", [128, 1024], F32))
            psB = s5.enter_context(nc.psum_tensor("p5_psB", [128, 1024], F32))
            psSub = s5.enter_context(nc.psum_tensor("p5_psSub", [128, 512], F32))
            psO = s5.enter_context(nc.psum_tensor("p5_psO", [128, 64], F32))
            psR = s5.enter_context(nc.psum_tensor("p5_psR", [128, 512], F32))
            psS2 = s5.enter_context(nc.psum_tensor("p5_psS2", [128, 512], F32))

            b.phase()
            xb = (xln_c0, xln_c1)
            wqk_free = [0, 0]
            d_wqk = [0, 0]

            def fetch_pair_w(pr):
                sl5 = pr % 2
                last = 0
                for i, srcw in enumerate((wsc["q1"], wsc["q1s"], wsc["k1"], wsc["k1s"])):
                    last = b.dma("scalar", wqk[sl5][:, i * 1024:(i + 1) * 1024],
                                 srcw[pr * 128:(pr + 1) * 128, :],
                                 waits=([(b.tT, wqk_free[sl5])] if i == 0 else ()),
                                 sem=b.sem(f"wq{sl5}"))
                d_wqk[sl5] = last

            fetch_pair_w(0)
            ring_exp = [0, 0]
            ring_P = [0, 0]
            ring_PT = [0, 0]
            v_negb_cur = [0, 0]
            v_rl_cur = [0, 0]
            a_exp_last = [0, 0]
            v_Oev_last = 0
            v_op_last = 0
            t_op_last = 0
            d_otr_last = 0
            nop = 0
            st_hist = [None] * 32
            st_ring = [(b.dS, 0)] * 4
            v_rcread = [0, 0]
            v_rope_k = 0
            v_rope_q = 0
            d_tab = [0, 0]

            for g in range(4):
                d_wv = 0
                for dc in range(8):
                    d_wv = b.dma("scalar", wvb[:, dc * 256:(dc + 1) * 256],
                                 wsc["v1"][dc * 128:(dc + 1) * 128, g * 256:(g + 1) * 256],
                                 waits=[(b.tT, b.tT.n)] if (g > 0 and dc == 0) else (),
                                 sem=b.sem("wv"))
                vv = b.tV.n
                for tc in range(32):
                    pe = (psR, psS2)[tc % 2]
                    w = [(b.sem("wv"), d_wv), (b.sem("ago"), d_ago),
                         (b.tV, vv if tc >= 2 else v_op_last)]
                    for dc in range(8):
                        tv = b.mm(pe[:, 0:256],
                                  xb[tc // 16][:, dc * TH + (tc % 16) * 128:
                                               dc * TH + (tc % 16) * 128 + 128],
                                  wvb[:, dc * 256:(dc + 1) * 256],
                                  start=(dc == 0), stop=(dc == 7),
                                  waits=(w if dc == 0 else ()), inc=(dc == 7))
                    vv = b.vcopy(v_sb[:, tc * 256:(tc + 1) * 256], pe[:, 0:256],
                                 waits=[(b.tT, tv)])
                v_V = vv

                for pr in (2 * g, 2 * g + 1):
                    sl5 = pr % 2
                    if pr + 1 < 8:
                        fetch_pair_w(pr + 1)
                    for tb in range(8):
                        tsem = b.sem(f"tb{tb % 2}")
                        d_ct = b.dma("scalar", ctab[tb % 2][:, :], ck_t[:, tb * 512:(tb + 1) * 512],
                                     waits=[(b.tV, d_tab[tb % 2])], sem=tsem)
                        d_tb = b.dma("scalar", stab[tb % 2][:, :], sk_t[:, tb * 512:(tb + 1) * 512],
                                     sem=tsem)
                        w = [(b.sem(f"wq{sl5}"), d_wqk[sl5]), (b.sem("ago"), d_ago),
                             (b.tV, b.tV.n if tb == 0 else v_rope_k)]
                        for ic in range(8):
                            b.mm(psR[:, :], wqk[sl5][:, 2 * 1024 + ic * 128: 2 * 1024 + (ic + 1) * 128],
                                 xb[tb // 4][:, ic * TH + (tb % 4) * 512: ic * TH + (tb % 4) * 512 + 512],
                                 start=(ic == 0), stop=(ic == 7),
                                 waits=(w if ic == 0 else ()))
                            tk = b.mm(psS2[:, :], wqk[sl5][:, 3 * 1024 + ic * 128: 3 * 1024 + (ic + 1) * 128],
                                      xb[tb // 4][:, ic * TH + (tb % 4) * 512: ic * TH + (tb % 4) * 512 + 512],
                                      start=(ic == 0), stop=(ic == 7), inc=(ic == 7))
                        b.tt(rtmp[0][:, :], psR[:, :], ctab[tb % 2][:, :], ALU.mult,
                             waits=[(b.tT, tk), (b.sem(f"tb{tb % 2}"), d_tb)], inc=False)
                        b.tt(rtmp[1][:, :], psS2[:, :], stab[tb % 2][:, :], ALU.mult, inc=False)
                        v_rope_k = b.tt(k2[:, tb * 512:(tb + 1) * 512], rtmp[0][:, :],
                                        rtmp[1][:, :], ALU.add)
                        d_tab[tb % 2] = v_rope_k
                    for tb in range(4):
                        tsem = b.sem(f"tb{tb % 2}")
                        d_ct = b.dma("scalar", ctab[tb % 2][:, :], cq_t[:, tb * 512:(tb + 1) * 512],
                                     waits=[(b.tV, d_tab[tb % 2])], sem=tsem)
                        d_tb = b.dma("scalar", stab[tb % 2][:, :], sq_t[:, tb * 512:(tb + 1) * 512],
                                     sem=tsem)
                        w = [(b.sem(f"wq{sl5}"), d_wqk[sl5]),
                             (b.tV, v_rope_q if tb > 0 else v_rope_k)]
                        for ic in range(8):
                            b.mm(psR[:, :], wqk[sl5][:, ic * 128:(ic + 1) * 128],
                                 xln_own[:, ic * TH + tb * 512: ic * TH + (tb + 1) * 512],
                                 start=(ic == 0), stop=(ic == 7),
                                 waits=(w if ic == 0 else ()))
                            tq = b.mm(psS2[:, :], wqk[sl5][:, 1024 + ic * 128: 1024 + (ic + 1) * 128],
                                      xln_own[:, ic * TH + tb * 512: ic * TH + (tb + 1) * 512],
                                      start=(ic == 0), stop=(ic == 7), inc=(ic == 7))
                        b.tt(rtmp[0][:, :], psR[:, :], ctab[tb % 2][:, :], ALU.mult,
                             waits=[(b.tT, tq), (b.sem(f"tb{tb % 2}"), d_tb)], inc=False)
                        b.tt(rtmp[1][:, :], psS2[:, :], stab[tb % 2][:, :], ALU.mult, inc=False)
                        v_rope_q = b.tt(q2[:, tb * 512:(tb + 1) * 512], rtmp[0][:, :],
                                        rtmp[1][:, :], ALU.add)
                        d_tab[tb % 2] = v_rope_q

                    for go in range(2):
                        base = go * 64
                        tp = (base, 0)
                        voff = ((pr % 2) * 2 + go) * 64
                        for qt in range(16):
                            par = qt % 2
                            if "noest" not in flags:
                                w = [(b.tV, v_negb_cur[0]), (b.tV, v_negb_cur[1]),
                                     (b.tV, v_rope_q)]
                                t_sub = b.mm(psSub[:, :],
                                             q2[base:base + 64, qt * 128:(qt + 1) * 128],
                                             k2[base:base + 64, 0:T:8],
                                             start=True, stop=True, waits=w, inc=True, tp=tp)
                                vnb = b.vred(negb[:, par:par + 1], psSub[:, :], ALU.max, negate=True,
                                             waits=[(b.tT, t_sub), (b.tA, a_exp_last[par])])
                                b.ph.wait("vector", [(b.tV, vnb)])
                                v_negb_cur[par] = b.ts(negb2[:, par:par + 1], negb[:, par:par + 1],
                                                       -MARGIN, None, ALU.add)
                            else:
                                v_negb_cur[par] = b.ph.emit(
                                    "vector", "memset", (negb2[:, par:par + 1], -30.0), None,
                                    waits=[(b.tA, a_exp_last[par])], inc=b.tV)
                            aes = [0] * 4
                            for kbi in range(4):
                                psX = (psA, psB)[kbi % 2]
                                w = [(b.tA, ring_exp[kbi % 2])]
                                if "safe" in flags or "safe1" in flags:
                                    w += [(b.tA, b.tA.n), (b.tV, b.tV.n),
                                          (b.dS, b.dS.n), (b.dG, b.dG.n)]
                                for hf in range(2):
                                    tS = b.mm(psX[:, hf * 512:(hf + 1) * 512],
                                              q2[base:base + 64, qt * 128:(qt + 1) * 128],
                                              k2[base:base + 64, kbi * 1024 + hf * 512:
                                                 kbi * 1024 + (hf + 1) * 512],
                                              start=True, stop=True,
                                              waits=(w if hf == 0 else ()),
                                              inc=(hf == 1), tp=tp)
                                we = [(b.tT, tS), (b.tV, v_negb_cur[par]),
                                      (b.dT, ring_P[par]), (b.tV, v_rl_cur[par])]
                                if "safe" in flags or "safe2" in flags:
                                    we += [(b.tV, b.tV.n), (b.dT, b.dT.n)]
                                ae = b.act(Pb[par][:, kbi * 1024:(kbi + 1) * 1024], psX[:, :],
                                           AF.Exp,
                                           bias=(0.0 if "bias0" in flags
                                                 else negb2[:, par:par + 1]), scale=1.0,
                                           accum=lpart[:, 4 * par + kbi: 4 * par + kbi + 1],
                                           waits=we)
                                aes[kbi] = ae
                                ring_exp[kbi % 2] = ae
                            a_exp_last[par] = aes[3]
                            trw = [(b.tA, aes[3]), (b.tT, ring_PT[par])]
                            if "safe" in flags or "safe3" in flags:
                                trw += [(b.tT, b.tT.n), (b.tV, b.tV.n), (b.tA, b.tA.n)]
                            ring_P[par] = b.tr(
                                PTb[par][:].rearrange("p (kc q) -> p kc q", kc=32),
                                Pb[par][:], waits=trw)
                            vls = b.vred(lsum[:, par:par + 1], lpart[:, 4 * par:4 * par + 4],
                                         ALU.add, waits=[(b.tA, aes[3])])
                            b.ph.wait("vector", [(b.tV, vls)])
                            v_rl_cur[par] = b.vrecip(rlb[:, par:par + 1], lsum[:, par:par + 1])
                            wo_ = [(b.dT, ring_P[par]), (b.tV, v_Oev_last), (b.tV, v_V)]
                            if "safe" in flags or "safe4" in flags:
                                wo_ += [(b.tA, b.tA.n), (b.tV, b.tV.n), (b.dT, b.dT.n)]
                            for kc in range(32):
                                tO = b.mm(psO[:, :], PTb[par][:, kc * 128:(kc + 1) * 128],
                                          v_sb[:, kc * 256 + voff: kc * 256 + voff + 64],
                                          start=(kc == 0), stop=(kc == 31),
                                          waits=(wo_ if kc == 0 else ()), inc=(kc == 31))
                            ring_PT[par] = tO
                            oew = [(b.tT, tO), (b.tV, v_rl_cur[par]), (b.dT, d_otr_last)]
                            if "safe" in flags or "safe5" in flags:
                                oew += [(b.tT, b.tT.n), (b.tA, b.tA.n)]
                            b.ph.wait("vector", oew)
                            v_Oev_last = b.ts(o2[:, qt * 128 + go * 64: qt * 128 + go * 64 + 64],
                                              psO[:, :], rlb[:, par:par + 1], None, ALU.mult)
                    d_otr_last = b.tr(otb[:].rearrange("p (t q) -> p t q", t=16), o2[:, :],
                                      waits=[(b.tV, v_Oev_last), (b.tT, t_op_last)])
                    d_wo1 = b.dma("scalar", wo1b[:, :], wsc["o1"][pr * 128:(pr + 1) * 128, :],
                                  waits=[(b.tT, t_op_last)], sem=b.sem("wo"))
                    v_op_prev = [v_op_last, v_op_last]
                    for dc in range(8):
                        for tb in range(4):
                            i = dc * 4 + tb
                            pp = (psR, psS2)[i % 2]
                            tile = resd[dc * 128:(dc + 1) * 128, tb * 512:(tb + 1) * 512]
                            rsem = b.sem(f"rc{i % 2}")
                            ssem = b.sem(f"so{i % 2}")
                            wrc = [(b.tV, v_rcread[i % 2])]
                            wrc += (list(b.resd_w) if st_hist[i] is None
                                    else [st_hist[i]])
                            d_rc2 = b.dma("scalar", rescho[i % 2][:, :], tile, waits=wrc,
                                          sem=rsem)
                            w = [(b.sem("wo"), d_wo1), (b.dT, d_otr_last),
                                 (b.tV, v_op_prev[i % 2]), (b.tV, v_rope_q)]
                            top = b.mm(pp[:, :], wo1b[:, dc * 128:(dc + 1) * 128],
                                       otb[:, tb * 512:(tb + 1) * 512],
                                       start=True, stop=True, waits=w, inc=True)
                            vop = b.tt(optmp[nop % 2][:, :], rescho[i % 2][:, :], pp[:, :],
                                       ALU.add,
                                       waits=[(b.tT, top), (rsem, d_rc2),
                                              st_ring[nop % 2]])
                            v_op_prev[i % 2] = vop
                            v_rcread[i % 2] = vop
                            st_hist[i] = (ssem, b.dma("scalar", tile, optmp[nop % 2][:, :],
                                                      waits=[(b.tV, vop)], sem=ssem))
                            st_ring[nop % 2] = st_hist[i]
                            nop += 1
                    v_op_last = vop
                    t_op_last = top
                    wqk_free[sl5] = b.tT.n
                    if stop == "p5b" and pr == 0:
                        b.ph.wait("gpsimd", [(b.dS, b.dS.n), (b.dT, d_otr_last)])
                        b.dma("gpsimd", dbg_x0[:, 0:2048], o2[:, :])
                        b.dma("gpsimd", dbg_x0[:, 2048:4096], otb[:, :])
                        b.dma("gpsimd", dbg_x0[:, 4096:8192], k2[:, :])
                        b.dma("gpsimd", dbg_x0[:, 8192:10240], q2[:, :])
                        b.dma("gpsimd", dbg_x1[:, 0:8192], v_sb[:, :])
                        b.dma("gpsimd", dbg_x1[:, 8192:9216], wo1b[:, :])
                        b.dma("gpsimd", dbg_x1[:, 9216:9224], lpart[:, :])
                        b.dma("gpsimd", dbg_x1[:, 9224:9226], negb2[:, :])
                        b.dma("gpsimd", dbg_x1[:, 9226:9228], rlb[:, :])
                        b.dma("gpsimd", dbg_x1[:, 9228:9230], lsum[:, :])
                        b.dma("gpsimd", dbg_x1[:, 10240:14336], Pb[1][:, :])
                        b.dma("gpsimd", dbg_r[:, :], resd[:, :])
                        b.ph.wait("sync", [(b.dG, b.dG.n)])
                        b.endphase()
                        s5.close()
                        scopeC.close()
                        scopeB.close()
                        return nc, b.ninstr
            b.resd_w = [(b.sem("so0"), b.sem("so0").n),
                        (b.sem("so1"), b.sem("so1").n)]
            b.endphase()
        if stop == "p5":
            dump_stop()
            scopeC.close()
            scopeB.close()
            return nc, b.ninstr
        scopeC.close()

        ln_phase("p6", xln_own)

        with ExitStack() as s7:
            a_sb = s7.enter_context(nc.sbuf_tensor("p7_a", [128, 24 * 1024], BF16))
            wblk = [s7.enter_context(nc.sbuf_tensor(f"p7_w{i}", [128, 3072], BF16)) for i in range(2)]
            resch = [s7.enter_context(nc.sbuf_tensor(f"p7_rc{i}", [128, 1024], F32)) for i in range(2)]
            tmpf = [s7.enter_context(nc.sbuf_tensor(f"p7_tf{i}", [128, 1024], F32)) for i in range(2)]
            psF = [s7.enter_context(nc.psum_tensor(f"p7_ps{i}", [128, 1024], F32)) for i in range(2)]

            b.phase()
            t_blkfree = [0, 0]
            d_out = [0, 0]

            def mat7(src, nic, noc, rhs_of, evac, first_extra=()):
                ev_prev = [(b.tA, b.tA.n), (b.tV, b.tV.n)]
                for oc in range(noc):
                    slot = oc % 2
                    wsem = b.sem(f"wb{slot}")
                    dB = b.dma("scalar", wblk[slot][:, 0:nic * 128],
                               src[oc * 128:(oc + 1) * 128, 0:nic * 128],
                               waits=[(b.tT, t_blkfree[slot])], sem=wsem)
                    pf = psF[oc % 2]
                    w = [(wsem, dB)] + (list(first_extra) if oc == 0 else [])
                    if oc >= 2:
                        w.append(ev_prev[oc % 2])
                    else:
                        w += [(b.tA, b.tA.n), (b.tV, b.tV.n)]
                    for ic in range(nic):
                        for hf in range(2):
                            tm = b.mm(pf[:, hf * 512:(hf + 1) * 512],
                                      wblk[slot][:, ic * 128:(ic + 1) * 128],
                                      rhs_of(ic, hf),
                                      start=(ic == 0), stop=(ic == nic - 1),
                                      waits=(w if (ic == 0 and hf == 0) else ()),
                                      inc=(ic == nic - 1 and hf == 1))
                    t_blkfree[slot] = tm
                    ev_prev[oc % 2] = evac(oc, pf, tm)

            for p in range(2):
                tok = slice(p * 1024, (p + 1) * 1024)

                def x_rhs(ic, hf, p=p):
                    return xln_own[:, ic * TH + p * 1024 + hf * 512:
                                   ic * TH + p * 1024 + (hf + 1) * 512]

                def a_rhs(ic, hf):
                    return a_sb[:, ic * 1024 + hf * 512: ic * 1024 + (hf + 1) * 512]

                def ev_gate(oc, pf, tm):
                    a = b.act(a_sb[:, oc * 1024:(oc + 1) * 1024], pf[:, :], AF.Silu,
                              scale=C127, waits=[(b.tT, tm)])
                    return (b.tA, a)
                mat7(wsc["wg"], 8, 24, x_rhs, ev_gate)
                a_gate = b.tA.n

                def ev_up(oc, pf, tm):
                    v2 = b.tt(a_sb[:, oc * 1024:(oc + 1) * 1024],
                              a_sb[:, oc * 1024:(oc + 1) * 1024], pf[:, :], ALU.mult,
                              waits=[(b.tT, tm), (b.tA, a_gate)])
                    return (b.tV, v2)
                mat7(wsc["wu"], 8, 24, x_rhs, ev_up)
                v_up = b.tV.n

                d_rc = [0, 0]
                v_rc = [b.tV.n, b.tV.n]

                def ev_down(oc, pf, tm, tok=tok, p=p):
                    sl2 = oc % 2
                    b.ts(tmpf[sl2][:, :], pf[:, :], C127 * C127, None, ALU.mult,
                         waits=[(b.tT, tm), (b.sem(f"rc{sl2}"), d_rc[sl2])], inc=False)
                    v2 = b.tt(resch[sl2][:, :], resch[sl2][:, :], tmpf[sl2][:, :], ALU.add)
                    d_out[sl2] = b.dma("scalar", outT[oc * 128:(oc + 1) * 128, tok],
                                       resch[sl2][:, :], waits=[(b.tV, v2)],
                                       sem=b.sem(f"so{sl2}"))
                    v_rc[sl2] = v2
                    return (b.tV, v2)

                ev_prev = [(b.tA, b.tA.n), (b.tV, b.tV.n)]
                for oc in range(8):
                    sl2 = oc % 2
                    d_rc[sl2] = b.dma("scalar", resch[sl2][:, :],
                                      resd[oc * 128:(oc + 1) * 128, tok],
                                      waits=list(b.resd_w) + [(b.tV, v_rc[sl2]),
                                                              (b.sem(f"so{sl2}"), d_out[sl2])],
                                      sem=b.sem(f"rc{sl2}"))
                    slot = oc % 2
                    wsem = b.sem(f"wb{slot}")
                    dB = b.dma("scalar", wblk[slot][:, :],
                               wsc["wd"][oc * 128:(oc + 1) * 128, :],
                               waits=[(b.tT, t_blkfree[slot])], sem=wsem)
                    pf = psF[oc % 2]
                    w = [(wsem, dB), (b.tV, v_up), ev_prev[oc % 2]]
                    for ic in range(24):
                        for hf in range(2):
                            tm = b.mm(pf[:, hf * 512:(hf + 1) * 512],
                                      wblk[slot][:, ic * 128:(ic + 1) * 128],
                                      a_rhs(ic, hf),
                                      start=(ic == 0), stop=(ic == 23),
                                      waits=(w if (ic == 0 and hf == 0) else ()),
                                      inc=(ic == 23 and hf == 1))
                    t_blkfree[slot] = tm
                    ev_prev[oc % 2] = ev_down(oc, pf, tm)
            allsems = [(b.dS, b.dS.n), (b.dG, b.dG.n), (b.dT, b.dT.n)]
            allsems += [(s, s.n) for s in b._xsems.values()]
            b.ph.wait("sync", allsems)
            b.endphase()
        scopeB.close()

    return nc, b.ninstr


def kernel(**inputs):
    in_maps = _prep(inputs)
    nc, _ = _build()
    res = run_bass_kernel_spmd(nc, in_maps, core_ids=list(range(8)))
    out = np.empty((4, T, D), dtype=np.float32)
    for c in range(8):
        bb, hh = c // 2, c % 2
        out[bb, hh * TH:(hh + 1) * TH, :] = np.asarray(res.results[c]["outT"]).T
    return out
